# version 01
# baseline (speedup 1.0000x reference)
import numpy as np
import contextlib
import concourse.bass as bass
import concourse.mybir as mybir

F32 = mybir.dt.float32
BF16 = mybir.dt.bfloat16
AF = mybir.ActivationFunctionType
ALU = mybir.AluOpType
AX = mybir.AxisListType

H, HD, D, CDIM = 16, 64, 1024, 3072
TT = 512
NCH = TT // 128
LN8 = float(np.log(8.0))
MASKNEG = -57344.0


class Sem:
    def __init__(self, nc, es, name, owner=None):
        self.h = es.enter_context(nc.semaphore(name))
        self.v = 0
        self.owner = owner

    def inc(self, instr, amt=1):
        if self.owner is None:
            instr.then_inc(self.h, amt)
            self.v += amt
        return instr

    def fence(self):
        o = self.owner
        if o is not None and o.dirty:
            o.last.then_inc(self.h, 1)
            self.v += 1
            o.dirty = False
        return self.v

    def wait(self, eng, v=None):
        if v is None:
            v = self.fence()
        if v > 0:
            eng.wait_ge(self.h, v)


def host_prep(inputs, n_pre=4, n_own=4):
    x = np.asarray(inputs["L0_attn_x"], np.float32)
    W_qkv = np.asarray(inputs["L0_attn_W_qkv"], np.float32)
    conv_w = np.asarray(inputs["L0_attn_conv_w"], np.float32)
    W_z = np.asarray(inputs["L0_attn_W_z"], np.float32)
    W_b = np.asarray(inputs["L0_attn_W_b"], np.float32)
    W_a = np.asarray(inputs["L0_attn_W_a"], np.float32)
    dtb = np.asarray(inputs["L0_attn_dt_bias"], np.float32)
    alog = np.asarray(inputs["L0_attn_A_log"], np.float32)
    W_out = np.asarray(inputs["L0_attn_W_out"], np.float32)
    br = np.asarray(inputs["L0_ffn_bias_real"], np.float32)
    bi = np.asarray(inputs["L0_ffn_bias_imag"], np.float32)

    convdiag = np.zeros((24, 4, 128, 128), np.float32)
    idx = np.arange(128)
    for cc in range(24):
        for tap in range(4):
            convdiag[cc, tap, idx, idx] = conv_w[cc * 128 + idx, 0, tap]
    mneg = np.where(np.arange(128)[:, None] > np.arange(128)[None, :],
                    np.float32(MASKNEG), np.float32(0.0))
    ind16 = -np.repeat(np.eye(16, dtype=np.float32), 128, axis=1)
    permz = np.concatenate([np.arange(64) + (2 * pair + par) * 64
                            for par in (0, 1) for pair in range(8)])
    permD = np.empty(2048, np.int64)
    for h in range(16):
        h2, l = h // 8, h % 8
        ql, par = l // 2, l % 2
        dst = h2 * 1024 + par * 512 + ql * 128
        permD[dst:dst + 128] = np.arange(128) + h * 128
    blockones16 = np.zeros((8, 128, 16), np.float32)
    for cc in range(8):
        blockones16[cc, :64, 2 * cc] = 1.0
        blockones16[cc, 64:, 2 * cc + 1] = 1.0

    common = dict(
        wqkvT=np.ascontiguousarray(W_qkv.T),
        wzT=np.ascontiguousarray(W_z.T[:, permz]),
        woutT=np.ascontiguousarray(W_out.T[permz, :]),
        wabT=np.ascontiguousarray(np.concatenate(
            [W_a, np.zeros((16, 1024), np.float32), W_b], 0).T),
        convdiag=convdiag,
        i128=np.eye(128, dtype=np.float32),
        i16=np.eye(16, dtype=np.float32),
        itile4=np.tile(np.eye(128, dtype=np.float32), (1, 4)),
        masknegT=np.ascontiguousarray(mneg.T),
        ind48=np.tile(ind16, (3, 1)),
        blockones16=blockones16,
        ones3=np.ones((3, 2048), np.float32),
        biasri=np.concatenate([br, bi])[None, :],
        dtb=dtb, alog=alog,
    )
    def expand_p1(G1):
        a = np.transpose(G1[0], (1, 2, 0))
        a = np.broadcast_to(a[:, :, None, :, None], (4, 32, 16, 32, 16))
        return np.ascontiguousarray(a.reshape(4, 512, 512))
    def expand_p2(G2):
        a = np.transpose(G2[:, :, 0, :], (0, 2, 1))
        a = np.broadcast_to(a[:, None, :, None, :], (4, 32, 16, 32, 16))
        return np.ascontiguousarray(a.reshape(4, 512, 512))
    common["p1r"] = expand_p1(np.asarray(inputs["L0_ffn_G1_R"], np.float32))
    common["p2r"] = expand_p2(np.asarray(inputs["L0_ffn_G2_R"], np.float32))
    common["p1i"] = expand_p1(np.asarray(inputs["L0_ffn_G1_I"], np.float32))
    common["p2i"] = expand_p2(np.asarray(inputs["L0_ffn_G2_I"], np.float32))
    in_maps = []
    zeros_pre = np.zeros((n_pre * TT, 1024), np.float32)
    for core in range(8):
        b, half = core // 2, core % 2
        m = dict(common)
        m["xown"] = np.ascontiguousarray(x[b, half * 2048: half * 2048 + n_own * TT])
        if n_pre > 0:
            m["xpre"] = (np.ascontiguousarray(x[b, 0:n_pre * TT]) if half == 1
                         else zeros_pre)
        in_maps.append(m)
    return in_maps


def build_program(n_pre=4, n_own=4, dbg_taps=(), stage=99, drains=True):
    nc = bass.Bass("TRN2", target_bir_lowering=False, debug=False)
    es = contextlib.ExitStack()

    xown = nc.dram_tensor("xown", [n_own * TT, D], F32, kind="ExternalInput").ap()
    xpre = (nc.dram_tensor("xpre", [n_pre * TT, D], F32, kind="ExternalInput").ap()
            if n_pre > 0 else None)
    d_wqkvT = nc.dram_tensor("wqkvT", [D, CDIM], F32, kind="ExternalInput").ap()
    d_wzT = nc.dram_tensor("wzT", [D, D], F32, kind="ExternalInput").ap()
    d_woutT = nc.dram_tensor("woutT", [D, D], F32, kind="ExternalInput").ap()
    d_wabT = nc.dram_tensor("wabT", [D, 48], F32, kind="ExternalInput").ap()
    d_diag = nc.dram_tensor("convdiag", [24, 4, 128, 128], F32, kind="ExternalInput").ap()
    d_i128 = nc.dram_tensor("i128", [128, 128], F32, kind="ExternalInput").ap()
    d_i16 = nc.dram_tensor("i16", [16, 16], F32, kind="ExternalInput").ap()
    d_itile4 = nc.dram_tensor("itile4", [128, 512], F32, kind="ExternalInput").ap()
    d_mneg = nc.dram_tensor("masknegT", [128, 128], F32, kind="ExternalInput").ap()
    d_ind48 = nc.dram_tensor("ind48", [48, 2048], F32, kind="ExternalInput").ap()
    d_bones = nc.dram_tensor("blockones16", [8, 128, 16], F32, kind="ExternalInput").ap()
    d_ones3 = nc.dram_tensor("ones3", [3, 2048], F32, kind="ExternalInput").ap()
    d_bias = nc.dram_tensor("biasri", [1, 1024], F32, kind="ExternalInput").ap()
    d_dtb = nc.dram_tensor("dtb", [H], F32, kind="ExternalInput").ap()
    d_alog = nc.dram_tensor("alog", [H], F32, kind="ExternalInput").ap()
    d_p1r = nc.dram_tensor("p1r", [4, 512, 512], F32, kind="ExternalInput").ap()
    d_p2r = nc.dram_tensor("p2r", [4, 512, 512], F32, kind="ExternalInput").ap()
    d_p1i = nc.dram_tensor("p1i", [4, 512, 512], F32, kind="ExternalInput").ap()
    d_p2i = nc.dram_tensor("p2i", [4, 512, 512], F32, kind="ExternalInput").ap()
    y = nc.dram_tensor("y", [n_own * TT, D], F32, kind="ExternalOutput").ap()
    d_diagbf = nc.dram_tensor("diagbf", [24, 4, 128, 128], BF16, kind="Internal").ap()
    d_scr = nc.dram_tensor("scr", [8448], F32, kind="Internal").ap()

    taps = {}
    def tapd(name, shape, dtype=F32):
        if name in dbg_taps:
            taps[name] = nc.dram_tensor("tap_" + name, list(shape), dtype,
                                        kind="ExternalOutput").ap()

    tapd("lnx", (128, 1024)); tapd("lnxT", (128, 8 * TT), BF16)
    tapd("mixed", (128, 24 * 515), BF16); tapd("g", (16, TT))
    tapd("spnb", (16, TT)); tapd("A", (16, TT)); tapd("K2", (16, TT))
    tapd("Q2", (16, TT)); tapd("Bt", (16, TT)); tapd("At", (16, TT))
    tapd("Dsb", (128, 2048), BF16); tapd("Psb", (128, 2048), BF16)
    tapd("vtok", (128, 1024), BF16); tapd("ktil", (128, 1024), BF16)
    tapd("qtil", (128, 1024), BF16)
    tapd("och", (128, 1024)); tapd("gated", (128, 1024), BF16)
    tapd("Sall", (128, 512)); tapd("x2", (128, 1024))
    tapd("wbig", (128, 8 * 1024), BF16); tapd("ztap", (128, 1024))
    tapd("eaq", (128, 1024), BF16)

    sb = lambda name, shape, dt=BF16: es.enter_context(nc.sbuf_tensor(name, shape, dt))
    wqkvT = sb("s_wqkvT", [128, 8 * CDIM])
    wzT = sb("s_wzT", [128, 8 * D])
    woutT = sb("s_woutT", [128, 8 * D])
    wabT = sb("s_wabT", [128, 8 * 48])
    wbig = sb("s_wbig", [128, 8 * D])
    dwin = sb("s_dwin", [128, 2 * 512])
    i128b = sb("s_i128b", [128, 128])
    itile4 = sb("s_itile4", [128, 512])
    masknegT = sb("s_mnegT", [128, 128])
    i16f = sb("s_i16f", [16, 16], F32)
    bonesb = sb("s_bones", [128, 8 * 16])
    biasb = sb("s_biasb", [33, 512])
    onesrow = sb("s_ones", [33, 128])
    dt_sb = sb("s_dt", [16, 1], F32)
    negea = sb("s_negea", [16, 1], F32)
    alog_sb = sb("s_alog", [16, 1], F32)
    ones16f = sb("s_ones16", [16, 128], F32)
    xa = [sb("s_xa0", [128, D], F32), sb("s_xa1", [128, D], F32)]
    xr = [sb("s_xr0", [128, D], F32), sb("s_xr1", [128, D], F32)]
    lnx = [sb("s_lnx0", [128, D]), sb("s_lnx1", [128, D])]
    lnxT = sb("s_lnxT", [128, 8 * TT])
    mixed = sb("s_mixed", [128, 24 * 515])
    halo = sb("s_halo", [128, 24 * 3])
    z_chunk = sb("s_z", [128, 1024], F32)
    vtok = sb("s_vtok", [128, 1024])
    ktil = sb("s_ktil", [128, 1024])
    qtil = sb("s_qtil", [128, 1024])
    ektok = sb("s_ektok", [128, 16], F32)
    eaq = sb("s_eaq", [128, 1024])
    dsb_t = sb("s_dsb", [128, 2048])
    o2 = dsb_t[:, 0:1024]
    gatedc = sb("s_gatedc", [128, 1024])
    gatedT = sb("s_gatedT", [128, 1024])
    ln2c = gatedc
    ln2T = gatedT
    x2tc = [sb("s_x2a", [128, 1024], F32), sb("s_x2b", [128, 1024], F32)]
    k2s = dsb_t[:, 0:512]
    K2cm = sb("s_K2cm", [16, TT], F32)
    Q2cm = sb("s_Q2cm", [16, TT], F32)
    g_cm = sb("s_gcm", [16, TT], F32)
    A_cm = sb("s_Acm", [16, TT], F32)
    Bt_cm = sb("s_Btcm", [16, TT], F32)
    ek_cm = sb("s_ekcm", [16, 128], F32)
    gam = sb("s_gam", [16, NCH], F32)
    gamb = sb("s_gamb", [128, 8 * NCH], F32)
    r1s = sb("s_r1s", [16, TT], F32)
    r2s = sb("s_r2s", [16, TT], F32)
    spnb = r2s
    At_cm = K2cm
    ea_cm = Q2cm
    b3 = sb("s_b3", [51, TT])
    a3 = sb("s_a3", [48, TT])
    ind51 = [sb("s_i51A", [51, 2048])]
    S_all = sb("s_Sall", [128, 512], F32)
    S_bf = sb("s_Sbf", [128, 512])
    rsq = sb("s_rsq", [128, 16], F32)
    rsqb = sb("s_rsqb", [128, 16], F32)
    stat_m = sb("s_statm", [128, 1], F32)
    stat_v = sb("s_statv", [128, 1], F32)
    stat_t = sb("s_statt", [128, 1], F32)
    bnbuf = sb("s_bn", [128, 14], F32)
    cstp = sb("s_cst", [128, 4], F32)

    ps1 = es.enter_context(nc.psum_tensor("ps1", [128, 1024], F32))
    ps2 = es.enter_context(nc.psum_tensor("ps2", [128, 1024], F32))
    psS = es.enter_context(nc.psum_tensor("psS", [128, 1024], F32))
    psT = es.enter_context(nc.psum_tensor("psT", [128, 1024], BF16))

    sL = Sem(nc, es, "sL")
    sG = Sem(nc, es, "sG")
    sT = Sem(nc, es, "sT")
    sV = Sem(nc, es, "sV")
    sS = Sem(nc, es, "sS")
    sO = [Sem(nc, es, "sO0"), Sem(nc, es, "sO1")]
    s_xa = [Sem(nc, es, "sxa0"), Sem(nc, es, "sxa1")]
    s_xr = [Sem(nc, es, "sxr0"), Sem(nc, es, "sxr1")]
    s_dw = [Sem(nc, es, "sdw0"), Sem(nc, es, "sdw1")]
    s_i51 = Sem(nc, es, "si51")
    s_eaq = Sem(nc, es, "seaq")
    s_bnc = Sem(nc, es, "sbnc")
    s_spl = Sem(nc, es, "sspl")
    s_tap = Sem(nc, es, "stap")

    blk = es.enter_context(nc.Block())

    _DRAIN_OPS = {"activation", "copy", "add", "mul", "square", "sqrt",
                  "tensor_copy", "tensor_tensor", "tensor_scalar",
                  "scalar_tensor_tensor", "tensor_reduce", "bn_stats",
                  "bn_aggr", "memset", "tensor_tensor_scan", "reciprocal",
                  "select", "transpose", "sign"}
    _TRACK_OPS = _DRAIN_OPS | {"matmul", "ldweights"}

    class EngProxy:
        def __init__(self, eng, drain):
            self._e = eng
            self._drain = drain
            self.last = None
            self.dirty = False

        def __getattr__(self, n):
            a = getattr(self._e, n)
            if n in _TRACK_OPS:
                def wrapped(*args, **kw):
                    if self._drain:
                        self._e.drain()
                    r = a(*args, **kw)
                    self.last = r
                    self.dirty = True
                    return r
                return wrapped
            return a

    G_, SY = nc.gpsimd, nc.sync
    T = EngProxy(nc.tensor, drain=False)
    V_ = EngProxy(nc.vector, drain=drains)
    S_ = EngProxy(nc.scalar, drain=drains)
    sT.owner = T
    sV.owner = V_
    sS.owner = S_

    def tap(name, src_ap, eng=None):
        if name not in taps:
            return
        e = eng or G_
        sV.wait(e); sS.wait(e); sT.wait(e)
        s_tap.inc(e.dma_start(out=taps[name], in_=src_ap), 16)
        s_tap.wait(e)
        s_tap.wait(V_); s_tap.wait(S_); s_tap.wait(T)

    sG.inc(G_.dma_start(out=wqkvT[:].rearrange("p (dc c) -> p dc c", dc=8),
                        in_=d_wqkvT.rearrange("(dc p) c -> p dc c", p=128)), 16)
    sG.inc(G_.dma_start(out=wzT[:].rearrange("p (dc c) -> p dc c", dc=8),
                        in_=d_wzT.rearrange("(dc p) c -> p dc c", p=128)), 16)
    sG.inc(G_.dma_start(out=woutT[:].rearrange("p (dc c) -> p dc c", dc=8),
                        in_=d_woutT.rearrange("(dc p) c -> p dc c", p=128)), 16)
    sG.inc(G_.dma_start(out=wabT[:].rearrange("p (dc c) -> p dc c", dc=8),
                        in_=d_wabT.rearrange("(dc p) c -> p dc c", p=128)), 16)
    sG.inc(G_.dma_start(out=i128b[:], in_=d_i128), 16)
    sG.inc(G_.dma_start(out=itile4[:], in_=d_itile4), 16)
    sG.inc(G_.dma_start(out=masknegT[:], in_=d_mneg), 16)
    sG.inc(G_.dma_start(out=bonesb[:].rearrange("p (cc h) -> p cc h", cc=8),
                        in_=d_bones.rearrange("cc p h -> p cc h")), 16)
    sG.inc(G_.dma_start(out=biasb[0:33:32, :], in_=d_bias.rearrange("o (two c) -> (o two) c", two=2)), 16)
    sG.inc(G_.dma_start(out=ind51[0][0:48, :], in_=d_ind48), 16)
    sG.inc(G_.dma_start(out=b3[48:51, :], in_=d_ones3[:, 0:TT]), 16)
    sL.inc(SY.dma_start(out=i16f[:], in_=d_i16), 16)
    sL.inc(SY.dma_start(out=dt_sb[:], in_=d_dtb.rearrange("(h o) -> h o", o=1)), 16)
    sL.inc(SY.dma_start(out=alog_sb[:], in_=d_alog.rearrange("(h o) -> h o", o=1)), 16)

    sL.wait(S_)
    sS.inc(S_.activation(out=negea[:], in_=alog_sb[:], func=AF.Exp))
    sS.wait(V_)
    sG.wait(V_)
    sV.inc(V_.tensor_scalar(out=negea[:], in0=negea[:], scalar1=-1.0,
                            scalar2=None, op0=ALU.mult))
    for ci, cv in enumerate([1e-5, 1.0, 1e-6, -LN8]):
        V_.memset(cstp[:, ci:ci + 1], cv)
        nc.const_aps.register(F32, cv, cstp[:, ci:ci + 1])
    V_.memset(S_all[:], 0.0)
    V_.memset(S_bf[:], 0.0)
    V_.memset(onesrow[:], 1.0)
    V_.memset(halo[:], 0.0)
    V_.memset(mixed[:], 0.0)
    sV.inc(V_.memset(ones16f[:], 1.0))

    diag_flat = d_diag.rearrange("cc tap p c -> (cc tap) p c")
    diagbf_flat = d_diagbf.rearrange("cc tap p c -> (cc tap) p c")
    for grp in range(6):
        p0 = grp * 16
        sV.wait(G_)
        sG.inc(G_.dma_start(
            out=dsb_t[:].rearrange("p (f c) -> p f c", f=16),
            in_=diag_flat[p0:p0 + 16].rearrange("f p c -> p f c")), 16)
        G_.wait_ge(sG.h, sG.v)
        sG.inc(G_.dma_start(
            out=diagbf_flat[p0:p0 + 16].rearrange("f p c -> p f c"),
            in_=dsb_t[:].rearrange("p (f c) -> p f c", f=16)), 16)
        G_.wait_ge(sG.h, sG.v)

    ffnP1 = z_chunk[:, 0:512]
    ffnP2 = z_chunk[:, 512:1024]
    ffnACC = x2tc[0][:, 0:512]
    ffnTMP = x2tc[0][:, 512:1024]
    for (dp1, dp2, kind) in ((d_p1r, d_p2r, "r"), (d_p1i, d_p2i, "i")):
        for rc in range(4):
            for cidx in range(4):
                sV.wait(G_)
                sG.inc(G_.dma_start(out=ffnP1,
                                    in_=dp1[cidx, rc * 128:(rc + 1) * 128, :]), 16)
                sG.inc(G_.dma_start(out=ffnP2,
                                    in_=dp2[cidx, rc * 128:(rc + 1) * 128, :]), 16)
                sG.wait(V_)
                if cidx == 0:
                    sV.inc(V_.tensor_tensor(out=ffnACC, in0=ffnP1, in1=ffnP2,
                                            op=ALU.mult))
                else:
                    sV.inc(V_.tensor_tensor(out=ffnTMP, in0=ffnP1, in1=ffnP2,
                                            op=ALU.mult))
                    sV.inc(V_.tensor_tensor(out=ffnACC, in0=ffnACC,
                                            in1=ffnTMP, op=ALU.add))
            if kind == "r":
                sV.inc(V_.tensor_copy(out=wbig[:, rc * 1024:rc * 1024 + 512],
                                      in_=ffnACC))
                sV.inc(V_.tensor_copy(out=wbig[:, (4 + rc) * 1024 + 512:(4 + rc) * 1024 + 1024],
                                      in_=ffnACC))
            else:
                sV.inc(V_.tensor_copy(out=wbig[:, rc * 1024 + 512:rc * 1024 + 1024],
                                      in_=ffnACC))
                sV.inc(V_.tensor_scalar(out=wbig[:, (4 + rc) * 1024:(4 + rc) * 1024 + 512],
                                        in0=ffnACC, scalar1=-1.0, scalar2=None,
                                        op0=ALU.mult))
    tap("wbig", wbig[:])

    total_tiles = (n_pre + n_own) if stage >= 2 else 0
    xa_free = [0, 0]
    xr_free = [0, 0]
    diag_loads = [0, 0]

    for tile in range(total_tiles):
        pre = tile < n_pre
        last_pre = pre and (tile == n_pre - 1)
        xsrc = xpre if pre else xown
        row0 = (tile if pre else tile - n_pre) * TT
        cc_list = list(range(8, 24)) if pre else list(range(24))

        for tc in range(4):
            SY.wait_ge(sV.h, xa_free[tc % 2])
            s_xa[tc % 2].inc(SY.dma_start(out=xa[tc % 2][:],
                                in_=xsrc[row0 + tc * 128: row0 + (tc + 1) * 128]), 16)
            xs = xa[tc % 2][:]
            s_xa[tc % 2].wait(V_)
            sT.wait(V_)
            V_.bn_stats(out=bnbuf[:, 0:6], in_=xs[:, 0:512])
            V_.bn_stats(out=bnbuf[:, 6:12], in_=xs[:, 512:1024])
            sV.inc(V_.bn_aggr(out=bnbuf[:, 12:14], in_=bnbuf[:, 0:12]))
            sV.wait(S_)
            S_.activation(out=stat_t[:], in_=bnbuf[:, 13:14], func=AF.Ln, bias=1e-5)
            sS.inc(S_.activation(out=stat_v[:], in_=stat_t[:], func=AF.Exp, scale=-0.5))
            sS.wait(V_)
            lx = lnx[tc % 2]
            sV.inc(V_.scalar_tensor_tensor(
                out=lx[:], in0=xs, scalar=bnbuf[:, 12:13], op0=ALU.subtract,
                in1=stat_v[:].broadcast_to([128, D]), op1=ALU.mult))
            xa_free[tc % 2] = sV.fence()
            if tc == 0:
                tap("lnx", lx[:])
            sV.wait(T)
            for dc in range(8):
                sT.inc(T.transpose(out=psT[:, dc * 128:(dc + 1) * 128],
                                   in_=lx[:, dc * 128:(dc + 1) * 128],
                                   identity=i128b[:]))
            sT.wait(V_)
            sV.inc(V_.tensor_copy(
                out=lnxT[:].rearrange("p (dc t) -> p dc t", dc=8)[:, :, tc * 128:(tc + 1) * 128],
                in_=psT[:].rearrange("p (dc t) -> p dc t", dc=8)))
            sV.wait(T)
        tap("lnxT", lnxT[:])

        evac_cnt = {}
        for i_cc, cc in enumerate(cc_list):
            q = i_cc % 2
            pq = ps1[:, q * 512:(q + 1) * 512]
            if i_cc >= 2:
                w = evac_cnt[i_cc - 2]
                T.wait_ge(w[0].h, w[1])
            for dc in range(8):
                sT.inc(T.matmul(pq, wqkvT[:, dc * CDIM + cc * 128: dc * CDIM + (cc + 1) * 128],
                                lnxT[:, dc * TT:(dc + 1) * TT],
                                start=(dc == 0), stop=(dc == 7)))
            eng, sem = (V_, sV) if (i_cc % 2 == 0) else (S_, sS)
            sT.wait(eng)
            if eng is V_:
                sem.inc(eng.tensor_copy(out=mixed[:, cc * 515 + 3: cc * 515 + 515], in_=pq))
            else:
                sem.inc(eng.copy(out=mixed[:, cc * 515 + 3: cc * 515 + 515], in_=pq))
            evac_cnt[i_cc] = (sem, sem.fence())
        if last_pre:
            sV.wait(T)
            sS.wait(T)
            for cc in range(8):
                q2h = ps1[:, 0:128] if cc % 2 == 0 else ps1[:, 512:640]
                if cc >= 2:
                    w = evac_cnt[("qh", cc - 2)]
                    T.wait_ge(w[0].h, w[1])
                for dc in range(8):
                    sT.inc(T.matmul(q2h, wqkvT[:, dc * CDIM + cc * 128: dc * CDIM + (cc + 1) * 128],
                                    lnxT[:, dc * TT + 384: dc * TT + 512],
                                    start=(dc == 0), stop=(dc == 7)))
                sT.wait(V_)
                sV.inc(V_.tensor_copy(out=mixed[:, cc * 515 + 3 + 384: cc * 515 + 515], in_=q2h))
                evac_cnt[("qh", cc)] = (sV, sV.fence())
        sS.wait(V_)
        sV.inc(V_.tensor_copy(
            out=halo[:].rearrange("p (cc c) -> p cc c", cc=24),
            in_=mixed[:].rearrange("p (cc c) -> p cc c", cc=24)[:, :, 512:515]))

        sV.wait(T)
        sS.wait(T)
        for dc in range(8):
            sT.inc(T.matmul(psS[0:48, 0:512], wabT[:, dc * 48:(dc + 1) * 48],
                            lnxT[:, dc * TT:(dc + 1) * TT],
                            start=(dc == 0), stop=(dc == 7)))
        sT.wait(S_)
        S_.activation(out=r2s[:], in_=psS[0:16, 0:512], func=AF.Exp, bias=dt_sb[:])
        sS.inc(S_.activation(out=r1s[:], in_=r2s[:], func=AF.Ln, bias=1.0))
        sS.wait(V_)
        sV.inc(V_.tensor_scalar(out=g_cm[:], in0=r1s[:], scalar1=negea[:],
                                scalar2=None, op0=ALU.mult))
        tap("g", g_cm[:])
        sV.wait(S_)
        S_.activation(out=r1s[:], in_=psS[32:48, 0:512], func=AF.Exp, scale=-1.0)
        sS.inc(S_.activation(out=spnb[:], in_=r1s[:], func=AF.Ln, bias=1.0))
        sS.wait(T)
        tap("spnb", spnb[:])

        for i_cc, cc in enumerate(cc_list):
            dhalf = i_cc % 2
            if i_cc >= 2:
                w = evac_cnt.get(("conv", i_cc - 2))
                SY.wait_ge(w[0].h, w[1])
            s_dw[dhalf].inc(SY.dma_start(
                out=dwin[:, dhalf * 512:(dhalf + 1) * 512].rearrange("p (f c) -> p f c", f=4),
                in_=diagbf_flat[cc * 4:(cc + 1) * 4].rearrange("f p c -> p f c")), 16)
            s_dw[dhalf].wait(T)
            sV.wait(T)
            sS.wait(T)
            ph = ps2[:, dhalf * 512:(dhalf + 1) * 512]
            for tapn in range(4):
                sT.inc(T.matmul(ph, dwin[:, dhalf * 512 + tapn * 128: dhalf * 512 + (tapn + 1) * 128],
                                mixed[:, cc * 515 + tapn: cc * 515 + tapn + 512],
                                start=(tapn == 0), stop=(tapn == 3)))
            evac_cnt[("conv", i_cc)] = (sT, sT.fence())
            sT.wait(S_)
            sS.inc(S_.activation(out=mixed[:, cc * 515 + 3: cc * 515 + 515], in_=ph,
                                 func=AF.Silu))
        sS.wait(V_)
        sT.wait(V_)
        sV.inc(V_.tensor_copy(
            out=mixed[:].rearrange("p (cc c) -> p cc c", cc=24)[:, :, 0:3],
            in_=halo[:].rearrange("p (cc c) -> p cc c", cc=24)))
        tap("mixed", mixed[:])

        if stage < 3:
            continue
        def sumsq(cc0, dest_cm, tapname):
            sS.wait(T)
            sV.wait(T)
            for i in range(8):
                cc = cc0 + i
                sS.wait(V_)
                if i > 0:
                    sT.wait(V_)
                sV.inc(V_.tensor_tensor(out=k2s[:], in0=mixed[:, cc * 515 + 3: cc * 515 + 515],
                                        in1=mixed[:, cc * 515 + 3: cc * 515 + 515], op=ALU.mult))
                sV.wait(T)
                sT.inc(T.matmul(psS[0:16, 0:512], bonesb[:, i * 16:(i + 1) * 16], k2s[:],
                                start=(i == 0), stop=(i == 7)))
            sT.wait(V_)
            sV.inc(V_.tensor_copy(out=dest_cm[:], in_=psS[0:16, 0:512]))
            tap(tapname, dest_cm[:])
        sumsq(8, K2cm, "K2")
        if not pre:
            sumsq(0, Q2cm, "Q2")

        sS.wait(V_); sG.wait(V_)
        for ch in range(NCH):
            sV.inc(V_.tensor_tensor_scan(out=A_cm[:, ch * 128:(ch + 1) * 128],
                                         data0=ones16f[:], data1=g_cm[:, ch * 128:(ch + 1) * 128],
                                         initial=0.0, op0=ALU.mult, op1=ALU.add))
        tap("A", A_cm[:])
        sV.wait(S_)
        sS.inc(S_.activation(out=r1s[:], in_=K2cm[:], func=AF.Ln, bias=1e-6))
        sS.wait(V_)
        V_.scalar_tensor_tensor(out=Bt_cm[:], in0=r1s[:], scalar=0.5, op0=ALU.mult,
                                in1=A_cm[:], op1=ALU.add)
        sV.inc(V_.tensor_tensor(out=Bt_cm[:], in0=Bt_cm[:], in1=spnb[:], op=ALU.add))
        tap("Bt", Bt_cm[:])
        if not pre:
            sV.wait(S_)
            sS.inc(S_.activation(out=r2s[:], in_=Q2cm[:], func=AF.Ln, bias=1e-6))
            sS.wait(V_)
            sV.inc(V_.scalar_tensor_tensor(out=At_cm[:], in0=r2s[:], scalar=-0.5,
                                           op0=ALU.mult, in1=A_cm[:], op1=ALU.add))
            tap("At", At_cm[:])
        gcol_slice = A_cm[:].rearrange("h (ch t) -> h ch t", ch=NCH)[:, :, 127:128]
        sV.wait(S_)
        sS.inc(S_.activation(out=gam[:], in_=gcol_slice.rearrange("h ch o -> h (ch o)"),
                             func=AF.Exp))
        tmpA = dsb_t[0:16, 0:512]
        tmpB = dsb_t[0:16, 512:1024]
        def split48(srcx, dst, rscr):
            sS.wait(V_)
            sT.wait(V_)
            sV.inc(V_.tensor_copy(out=dst[0:16, :], in_=srcx[:]))
            sV.inc(V_.scalar_tensor_tensor(out=rscr[:], in0=srcx[:], scalar=1.0,
                                           op0=ALU.mult, in1=dst[0:16, :], op1=ALU.subtract))
            sV.inc(V_.tensor_copy(out=tmpA, in_=rscr[:]))
            sV.inc(V_.scalar_tensor_tensor(out=rscr[:], in0=rscr[:], scalar=1.0,
                                           op0=ALU.mult, in1=tmpA, op1=ALU.subtract))
            sV.inc(V_.tensor_copy(out=tmpB, in_=rscr[:]))
            sV.wait(G_)
            s_spl.inc(G_.dma_start(out=dst[16:32, :], in_=tmpA), 16)
            s_spl.inc(G_.dma_start(out=dst[32:48, :], in_=tmpB), 16)
            s_spl.wait(V_)
            s_spl.wait(T)
        split48(Bt_cm, b3, r1s)
        if not pre:
            split48(At_cm, a3, r1s)
            sV.wait(S_)
            sS.inc(S_.activation(out=ea_cm[:], in_=At_cm[:], func=AF.Exp, bias=-LN8))
            sS.wait(G_)
            s_bnc.inc(G_.dma_start(out=d_scr[0:8192].rearrange("(h t) -> h t", h=16),
                                   in_=ea_cm[:]), 16)
        sS.wait(G_)
        sV.wait(G_)
        s_bnc.inc(G_.dma_start(out=d_scr[8192:8192 + 16 * NCH].rearrange("(h c) -> h c", h=16),
                               in_=gam[:]), 16)
        s_bnc.wait(G_)
        gfull = d_scr[8192:8192 + 16 * NCH].rearrange("(h ch) -> h ch", h=16)
        for ph in range(2):
            gsrc = gfull[ph:16:2][None, :, :].broadcast_to([64, 8, NCH])
            s_bnc.inc(G_.dma_start(
                out=gamb[ph * 64:(ph + 1) * 64, :].rearrange("p (pair ch) -> p pair ch", pair=8),
                in_=gsrc), 16)
        s_bnc.wait(V_)

        if stage < 4:
            continue
        for ch in range(NCH):
            tc = ch
            col = ch * 128
            if not pre:
                SY.wait_ge(sV.h, xr_free[ch % 2])
                s_xr[ch % 2].inc(SY.dma_start(out=xr[ch % 2][:],
                                    in_=xsrc[row0 + tc * 128: row0 + (tc + 1) * 128]), 16)
            if not pre:
                i51 = ind51[0]
                sV.wait(G_)
                sT.wait(G_)
                G_.wait_ge(s_spl.h, s_spl.v)
                for s3 in range(3):
                    s_i51.inc(G_.dma_start(
                        out=i51[48 + s3:49 + s3, :].rearrange("o (h c) -> o h c", h=16),
                        in_=a3[16 * s3:16 * (s3 + 1), col:col + 128]), 16)
                efull = d_scr[0:8192].rearrange("(h t) -> h t", h=16)
                G_.wait_ge(s_bnc.h, s_bnc.v)
                for ph in range(2):
                    esrc = efull[ph:16:2, col:col + 128]
                    esrc = esrc[None, :, :].broadcast_to([64, 8, 128])
                    s_eaq.inc(G_.dma_start(
                        out=eaq[ph * 64:(ph + 1) * 64, :].rearrange("p (cc i) -> p cc i", cc=8),
                        in_=esrc), 16)

            sV.wait(T)
            for vcc in range(8):
                cc = 16 + vcc
                sT.inc(T.transpose(out=psT[:, vcc * 128:(vcc + 1) * 128],
                                   in_=mixed[:, cc * 515 + 3 + col: cc * 515 + 3 + col + 128],
                                   identity=i128b[:]))
            sT.wait(V_)
            sV.inc(V_.tensor_copy(out=vtok[:], in_=psT[:]))
            if ch == 0:
                tap("vtok", vtok[:])
            sV.wait(S_)
            sS.inc(S_.activation(out=ek_cm[:], in_=Bt_cm[:, col:col + 128], func=AF.Exp,
                                 scale=-1.0,
                                 bias=A_cm[:].rearrange("h (c t) -> h c t", c=NCH)[:, ch, 127:128]))
            sS.wait(T)
            sV.wait(T)
            sT.inc(T.matmul(psS[:, 0:16], ek_cm[:], i16f[:], start=True, stop=True))
            sT.wait(V_)
            sV.inc(V_.tensor_copy(out=ektok[:], in_=psS[:, 0:16]))
            sV.wait(T)
            for kcc in range(8):
                cc = 8 + kcc
                sT.inc(T.transpose(out=psT[:, kcc * 128:(kcc + 1) * 128],
                                   in_=mixed[:, cc * 515 + 3 + col: cc * 515 + 3 + col + 128],
                                   identity=i128b[:]))
            sT.wait(V_)
            sV.inc(V_.tensor_tensor(
                out=ktil[:].rearrange("p (h w) -> p h w", w=64),
                in0=psT[:].rearrange("p (h w) -> p h w", w=64),
                in1=ektok[:, :, None].broadcast_to([128, 16, 64]),
                op=ALU.mult))
            if ch == 0:
                tap("ktil", ktil[:])

            if stage < 41:
                continue
            if not pre:
                s_eaq.wait(V_)
                qsl = mixed[:].rearrange("p (cc c) -> p cc c", cc=24)[:, 0:8, 3 + col: 3 + col + 128]
                sV.inc(V_.tensor_tensor(out=qtil[:].rearrange("p (cc i) -> p cc i", cc=8),
                                        in0=qsl, in1=eaq[:].rearrange("p (cc i) -> p cc i", cc=8),
                                        op=ALU.mult))
                if ch == 0:
                    tap("qtil", qtil[:])
                if stage < 42:
                    continue
                s_i51.wait(T)
                for h2 in range(2):
                    sS.wait(T)
                    sV.wait(T)
                    ihalf = i51[:, h2 * 1024:(h2 + 1) * 1024].rearrange(
                        "k (ql two c) -> k two ql c", ql=4, two=2, c=128)
                    for par in range(2):
                        pq = ps1[:, par * 512:(par + 1) * 512]
                        sT.inc(T.matmul(pq, b3[:, col:col + 128],
                                        ihalf[:, par, :, :],
                                        start=True, stop=False))
                        sT.inc(T.matmul(pq, masknegT[:], itile4[:],
                                        start=False, stop=True))
                    if stage >= 43:
                        sT.wait(S_)
                        sS.inc(S_.activation(out=dsb_t[:, h2 * 1024:(h2 + 1) * 1024],
                                             in_=ps1[:], func=AF.Exp, bias=-LN8))
                    if stage >= 44:
                        sS.wait(T)
                        for ql in range(4):
                            qcc = h2 * 4 + ql
                            kl = mixed[:, (8 + qcc) * 515 + 3 + col: (8 + qcc) * 515 + 3 + col + 128]
                            qll = mixed[:, qcc * 515 + 3 + col: qcc * 515 + 3 + col + 128]
                            sT.inc(T.matmul(ps1[:, ql * 128:(ql + 1) * 128],
                                            kl[0:64, :], qll[0:64, :],
                                            start=True, stop=True, tile_position=(0, 0)))
                            sT.inc(T.matmul(ps1[:, 512 + ql * 128:512 + (ql + 1) * 128],
                                            kl[64:128, :], qll[64:128, :],
                                            start=True, stop=True, tile_position=(64, 0)))
                    if stage >= 45:
                        sT.wait(V_)
                        sV.inc(V_.tensor_tensor(out=dsb_t[:, h2 * 1024:(h2 + 1) * 1024],
                                                in0=ps1[:],
                                                in1=dsb_t[:, h2 * 1024:(h2 + 1) * 1024],
                                                op=ALU.mult))
                if ch == 0:
                    tap("Dsb", dsb_t[:], eng=G_)
                    tap("Psb", dsb_t[:], eng=G_)
                if stage < 46:
                    continue
                sV.wait(T)
                for dc in range(8):
                    lslice = lnxT[:, dc * TT + col: dc * TT + col + 128]
                    sT.inc(T.matmul(ps2[:, 0:512], lslice, wzT[:, dc * D: dc * D + 512],
                                    start=(dc == 0), stop=(dc == 7)))
                    sT.inc(T.matmul(ps2[:, 512:1024], lslice, wzT[:, dc * D + 512: (dc + 1) * D],
                                    start=(dc == 0), stop=(dc == 7)))
                sT.wait(S_)
                sS.inc(S_.activation(out=z_chunk[:], in_=ps2[:], func=AF.Silu))
                if ch == 0:
                    tap("ztap", z_chunk[:])
                if stage < 47:
                    continue
                sS.wait(T)
                for hh in range(16):
                    par, pair = hh % 2, hh // 2
                    dcol = (hh // 8) * 1024 + par * 512 + ((hh % 8) // 2) * 128
                    sT.inc(T.matmul(ps2[:, par * 512 + pair * 64: par * 512 + (pair + 1) * 64],
                                    dsb_t[:, dcol:dcol + 128],
                                    vtok[:, hh * 64:(hh + 1) * 64],
                                    start=(hh < 2), stop=False))
                for qcc in range(8):
                    sT.inc(T.matmul(ps2[:, qcc * 64:(qcc + 1) * 64],
                                    qtil[0:64, qcc * 128:(qcc + 1) * 128],
                                    S_bf[0:64, qcc * 64:(qcc + 1) * 64],
                                    start=False, stop=(qcc == 7), tile_position=(0, 0)))
                    sT.inc(T.matmul(ps2[:, 512 + qcc * 64:512 + (qcc + 1) * 64],
                                    qtil[64:128, qcc * 128:(qcc + 1) * 128],
                                    S_bf[64:128, qcc * 64:(qcc + 1) * 64],
                                    start=False, stop=(qcc == 7),
                                    tile_position=(64, 0)))
                o_done = sT.v

            if stage < 48:
                continue
            sV.wait(T)
            for hh in range(16):
                tp = (0, 0) if hh % 2 == 0 else (0, 64)
                rb = 0 if hh % 2 == 0 else 64
                cb = 0 if hh % 2 == 0 else 512
                pair = hh // 2
                sT.inc(T.matmul(psS[rb:rb + 64, cb + pair * 64: cb + (pair + 1) * 64],
                                ktil[:, hh * 64:(hh + 1) * 64],
                                vtok[:, hh * 64:(hh + 1) * 64],
                                start=(hh < 2), stop=(hh >= 14), tile_position=tp))
            state_done = sT.v

            if stage < 49:
                continue
            if not pre:
                sT.wait(S_)
                sS.inc(S_.activation(out=o2, in_=ps2[:], func=AF.Square))
                sS.wait(V_)
                sV.inc(V_.tensor_reduce(out=rsq[:], in_=o2.rearrange("p (h w) -> p h w", w=64),
                                        axis=AX.X, op=ALU.add))
                sV.wait(S_)
                S_.activation(out=rsqb[:], in_=rsq[:], func=AF.Ln, scale=1.0 / 64, bias=1e-6)
                sS.inc(S_.activation(out=rsqb[:], in_=rsqb[:], func=AF.Exp, scale=-0.5))
                sS.wait(V_)
                V_.scalar_tensor_tensor(
                    out=gatedc[:].rearrange("p (h w) -> p h w", w=64),
                    in0=ps2[:].rearrange("p (h w) -> p h w", w=64),
                    scalar=1.0, op0=ALU.mult,
                    in1=rsqb[:, :, None].broadcast_to([128, 16, 64]),
                    op1=ALU.mult)
                sV.inc(V_.tensor_tensor(out=gatedc[:], in0=gatedc[:], in1=z_chunk[:],
                                        op=ALU.mult))
                if ch == 0:
                    tap("gated", gatedc[:])

            if stage < 50:
                continue
            sT.wait(V_)
            gslice = gamb[:, ch: 8 * NCH: NCH]
            V_.tensor_tensor(out=S_all[:].rearrange("p (pair w) -> p pair w", pair=8),
                             in0=S_all[:].rearrange("p (pair w) -> p pair w", pair=8),
                             in1=gslice[:, :, None].broadcast_to([128, 8, 64]),
                             op=ALU.mult)
            V_.tensor_tensor(out=S_all[0:64, :], in0=S_all[0:64, :],
                             in1=psS[0:64, 0:512], op=ALU.add)
            V_.tensor_tensor(out=S_all[64:128, :], in0=S_all[64:128, :],
                             in1=psS[64:128, 512:1024], op=ALU.add)
            sV.inc(V_.tensor_copy(out=S_bf[:], in_=S_all[:]))
            if pre and ch == NCH - 1 and tile == n_pre - 1:
                tap("Sall", S_all[:])

            if stage < 51:
                continue
            if not pre:
                sV.wait(T)
                for dc in range(8):
                    sT.inc(T.transpose(out=psT[:, dc * 128:(dc + 1) * 128],
                                       in_=gatedc[:, dc * 128:(dc + 1) * 128],
                                       identity=i128b[:]))
                sT.wait(V_)
                sV.inc(V_.tensor_copy(out=gatedT[:], in_=psT[:]))
                sV.wait(T)
                sS.wait(T)
                for dc in range(8):
                    gsl = gatedT[:, dc * 128:(dc + 1) * 128]
                    sT.inc(T.matmul(ps2[:, 0:512], gsl, woutT[:, dc * D: dc * D + 512],
                                    start=(dc == 0), stop=(dc == 7)))
                    sT.inc(T.matmul(ps2[:, 512:1024], gsl, woutT[:, dc * D + 512:(dc + 1) * D],
                                    start=(dc == 0), stop=(dc == 7)))
                x2 = x2tc[ch % 2]
                sO[ch % 2].wait(V_)
                s_xr[ch % 2].wait(V_)
                sT.wait(V_)
                sV.inc(V_.tensor_tensor(out=x2[:], in0=ps2[:], in1=xr[ch % 2][:],
                                        op=ALU.add))
                xr_free[ch % 2] = sV.fence()
                if ch == 0:
                    tap("x2", x2[:])
                if stage < 52:
                    continue
                sV.wait(V_)
                V_.bn_stats(out=bnbuf[:, 0:6], in_=x2[:, 0:512])
                V_.bn_stats(out=bnbuf[:, 6:12], in_=x2[:, 512:1024])
                sV.inc(V_.bn_aggr(out=bnbuf[:, 12:14], in_=bnbuf[:, 0:12]))
                sV.wait(S_)
                S_.activation(out=stat_t[:], in_=bnbuf[:, 13:14], func=AF.Ln, bias=1e-5)
                sS.inc(S_.activation(out=stat_v[:], in_=stat_t[:], func=AF.Exp, scale=-0.5))
                sS.wait(V_)
                sV.inc(V_.scalar_tensor_tensor(
                    out=ln2c[:], in0=x2[:], scalar=bnbuf[:, 12:13], op0=ALU.subtract,
                    in1=stat_v[:].broadcast_to([128, D]), op1=ALU.mult))
                sV.wait(T)
                for dc in range(8):
                    sT.inc(T.transpose(out=psT[:, dc * 128:(dc + 1) * 128],
                                       in_=ln2c[:, dc * 128:(dc + 1) * 128],
                                       identity=i128b[:]))
                sT.wait(V_)
                sV.inc(V_.tensor_copy(out=ln2T[:], in_=psT[:]))
                if stage < 53:
                    continue
                sV.wait(T)
                for dg in range(8):
                    lsl = ln2T[:, dg * 128:(dg + 1) * 128]
                    sT.inc(T.matmul(ps2[:, 0:512], lsl, wbig[:, dg * D: dg * D + 512],
                                    start=(dg == 0), stop=False))
                    sT.inc(T.matmul(ps2[:, 512:1024], lsl, wbig[:, dg * D + 512:(dg + 1) * D],
                                    start=(dg == 0), stop=False))
                sT.inc(T.matmul(ps2[:, 0:512], onesrow[0:1, :], biasb[0:1, :],
                                start=False, stop=True))
                sT.inc(T.matmul(ps2[:, 512:1024], onesrow[32:33, :], biasb[32:33, :],
                                start=False, stop=True))
                sT.wait(V_)
                sV.inc(V_.tensor_tensor(out=x2[:], in0=ps2[:], in1=x2[:], op=ALU.add))
                sV.wait(SY)
                sO[ch % 2].inc(SY.dma_start(
                    out=y[row0 + tc * 128: row0 + (tc + 1) * 128, :], in_=x2[:]), 16)

    sO[0].wait(SY)
    sO[1].wait(SY)
    G_.wait_ge(sG.h, sG.v)
    G_.wait_ge(s_tap.h, s_tap.v)
    G_.wait_ge(s_bnc.h, s_bnc.v)
    G_.wait_ge(s_spl.h, s_spl.v)
    G_.wait_ge(s_i51.h, s_i51.v)
    G_.wait_ge(s_eaq.h, s_eaq.v)
    SY.wait_ge(s_dw[0].h, s_dw[0].v)
    SY.wait_ge(s_dw[1].h, s_dw[1].v)
    SY.wait_ge(s_xa[0].h, s_xa[0].v)
    SY.wait_ge(s_xa[1].h, s_xa[1].v)
    SY.wait_ge(s_xr[0].h, s_xr[0].v)
    SY.wait_ge(s_xr[1].h, s_xr[1].v)
    V_._e.wait_ge(sL.h, sL.v)
    es.close()
    return nc, taps


from concourse.bass_utils import run_bass_kernel_spmd

N_PRE, N_OWN = 4, 4


def kernel(**inputs):
    in_maps = host_prep(inputs, n_pre=N_PRE, n_own=N_OWN)
    nc, _ = build_program(n_pre=N_PRE, n_own=N_OWN)
    res = run_bass_kernel_spmd(nc, in_maps, core_ids=list(range(8)))
    out = np.empty((4, 4096, 1024), np.float32)
    for core in range(8):
        b, half = core // 2, core % 2
        out[b, half * 2048:(half + 1) * 2048] = res.results[core]["y"]
    return out


# version 03 (final)
# speedup vs baseline: 1.0516x; 1.0516x over previous
import numpy as np
import contextlib
import concourse.bass as bass
import concourse.mybir as mybir

F32 = mybir.dt.float32
BF16 = mybir.dt.bfloat16
AF = mybir.ActivationFunctionType
ALU = mybir.AluOpType
AX = mybir.AxisListType

H, HD, D, CDIM = 16, 64, 1024, 3072
TT = 512
NCH = TT // 128
LN8 = float(np.log(8.0))
MASKNEG = -57344.0


class Sem:
    def __init__(self, nc, es, name, owner=None):
        self.h = es.enter_context(nc.semaphore(name))
        self.v = 0
        self.owner = owner

    def inc(self, instr, amt=1):
        if self.owner is None:
            instr.then_inc(self.h, amt)
            self.v += amt
        return instr

    def fence(self):
        o = self.owner
        if o is not None and o.dirty:
            o.last.then_inc(self.h, 1)
            self.v += 1
            o.dirty = False
        return self.v

    def wait(self, eng, v=None):
        if v is None:
            v = self.fence()
        if v > 0:
            eng.wait_ge(self.h, v)


def host_prep(inputs, n_pre=4, n_own=4):
    x = np.asarray(inputs["L0_attn_x"], np.float32)
    W_qkv = np.asarray(inputs["L0_attn_W_qkv"], np.float32)
    conv_w = np.asarray(inputs["L0_attn_conv_w"], np.float32)
    W_z = np.asarray(inputs["L0_attn_W_z"], np.float32)
    W_b = np.asarray(inputs["L0_attn_W_b"], np.float32)
    W_a = np.asarray(inputs["L0_attn_W_a"], np.float32)
    dtb = np.asarray(inputs["L0_attn_dt_bias"], np.float32)
    alog = np.asarray(inputs["L0_attn_A_log"], np.float32)
    W_out = np.asarray(inputs["L0_attn_W_out"], np.float32)
    br = np.asarray(inputs["L0_ffn_bias_real"], np.float32)
    bi = np.asarray(inputs["L0_ffn_bias_imag"], np.float32)

    convdiag = np.zeros((24, 4, 128, 128), np.float32)
    idx = np.arange(128)
    for cc in range(24):
        for tap in range(4):
            convdiag[cc, tap, idx, idx] = conv_w[cc * 128 + idx, 0, tap]
    mneg = np.where(np.arange(128)[:, None] > np.arange(128)[None, :],
                    np.float32(MASKNEG), np.float32(0.0))
    ind16 = -np.repeat(np.eye(16, dtype=np.float32), 128, axis=1)
    permz = np.concatenate([np.arange(64) + (2 * pair + par) * 64
                            for par in (0, 1) for pair in range(8)])
    permD = np.empty(2048, np.int64)
    for h in range(16):
        h2, l = h // 8, h % 8
        ql, par = l // 2, l % 2
        dst = h2 * 1024 + par * 512 + ql * 128
        permD[dst:dst + 128] = np.arange(128) + h * 128
    blockones16 = np.zeros((8, 128, 16), np.float32)
    for cc in range(8):
        blockones16[cc, :64, 2 * cc] = 1.0
        blockones16[cc, 64:, 2 * cc + 1] = 1.0

    common = dict(
        wqkvT=np.ascontiguousarray(W_qkv.T),
        wzT=np.ascontiguousarray(W_z.T[:, permz]),
        woutT=np.ascontiguousarray(W_out.T[permz, :]),
        wabT=np.ascontiguousarray(np.concatenate(
            [W_a, np.zeros((16, 1024), np.float32), W_b], 0).T),
        convdiag=convdiag,
        i128=np.eye(128, dtype=np.float32),
        i16=np.eye(16, dtype=np.float32),
        itile4=np.tile(np.eye(128, dtype=np.float32), (1, 4)),
        masknegT=np.ascontiguousarray(mneg.T),
        ind48=np.tile(ind16, (3, 1)),
        blockones16=blockones16,
        ones3=np.ones((3, 2048), np.float32),
        biasri=np.concatenate([br, bi])[None, :],
        dtb=dtb, alog=alog,
    )
    def expand_p1(G1):
        a = np.transpose(G1[0], (1, 2, 0))
        a = np.broadcast_to(a[:, :, None, :, None], (4, 32, 16, 32, 16))
        return np.ascontiguousarray(a.reshape(4, 512, 512))
    def expand_p2(G2):
        a = np.transpose(G2[:, :, 0, :], (0, 2, 1))
        a = np.broadcast_to(a[:, None, :, None, :], (4, 32, 16, 32, 16))
        return np.ascontiguousarray(a.reshape(4, 512, 512))
    common["p1r"] = expand_p1(np.asarray(inputs["L0_ffn_G1_R"], np.float32))
    common["p2r"] = expand_p2(np.asarray(inputs["L0_ffn_G2_R"], np.float32))
    common["p1i"] = expand_p1(np.asarray(inputs["L0_ffn_G1_I"], np.float32))
    common["p2i"] = expand_p2(np.asarray(inputs["L0_ffn_G2_I"], np.float32))
    in_maps = []
    zeros_pre = np.zeros((n_pre * TT, 1024), np.float32)
    for core in range(8):
        b, half = core // 2, core % 2
        m = dict(common)
        m["xown"] = np.ascontiguousarray(x[b, half * 2048: half * 2048 + n_own * TT])
        if n_pre > 0:
            m["xpre"] = (np.ascontiguousarray(x[b, 0:n_pre * TT]) if half == 1
                         else zeros_pre)
        in_maps.append(m)
    return in_maps


def build_program(n_pre=4, n_own=4, dbg_taps=(), stage=99, drains=True):
    nc = bass.Bass("TRN2", target_bir_lowering=False, debug=False)
    es = contextlib.ExitStack()

    xown = nc.dram_tensor("xown", [n_own * TT, D], F32, kind="ExternalInput").ap()
    xpre = (nc.dram_tensor("xpre", [n_pre * TT, D], F32, kind="ExternalInput").ap()
            if n_pre > 0 else None)
    d_wqkvT = nc.dram_tensor("wqkvT", [D, CDIM], F32, kind="ExternalInput").ap()
    d_wzT = nc.dram_tensor("wzT", [D, D], F32, kind="ExternalInput").ap()
    d_woutT = nc.dram_tensor("woutT", [D, D], F32, kind="ExternalInput").ap()
    d_wabT = nc.dram_tensor("wabT", [D, 48], F32, kind="ExternalInput").ap()
    d_diag = nc.dram_tensor("convdiag", [24, 4, 128, 128], F32, kind="ExternalInput").ap()
    d_i128 = nc.dram_tensor("i128", [128, 128], F32, kind="ExternalInput").ap()
    d_i16 = nc.dram_tensor("i16", [16, 16], F32, kind="ExternalInput").ap()
    d_itile4 = nc.dram_tensor("itile4", [128, 512], F32, kind="ExternalInput").ap()
    d_mneg = nc.dram_tensor("masknegT", [128, 128], F32, kind="ExternalInput").ap()
    d_ind48 = nc.dram_tensor("ind48", [48, 2048], F32, kind="ExternalInput").ap()
    d_bones = nc.dram_tensor("blockones16", [8, 128, 16], F32, kind="ExternalInput").ap()
    d_ones3 = nc.dram_tensor("ones3", [3, 2048], F32, kind="ExternalInput").ap()
    d_bias = nc.dram_tensor("biasri", [1, 1024], F32, kind="ExternalInput").ap()
    d_dtb = nc.dram_tensor("dtb", [H], F32, kind="ExternalInput").ap()
    d_alog = nc.dram_tensor("alog", [H], F32, kind="ExternalInput").ap()
    d_p1r = nc.dram_tensor("p1r", [4, 512, 512], F32, kind="ExternalInput").ap()
    d_p2r = nc.dram_tensor("p2r", [4, 512, 512], F32, kind="ExternalInput").ap()
    d_p1i = nc.dram_tensor("p1i", [4, 512, 512], F32, kind="ExternalInput").ap()
    d_p2i = nc.dram_tensor("p2i", [4, 512, 512], F32, kind="ExternalInput").ap()
    y = nc.dram_tensor("y", [n_own * TT, D], F32, kind="ExternalOutput").ap()
    d_diagbf = nc.dram_tensor("diagbf", [24, 4, 128, 128], BF16, kind="Internal").ap()
    d_scr = nc.dram_tensor("scr", [8448], F32, kind="Internal").ap()

    taps = {}
    def tapd(name, shape, dtype=F32):
        if name in dbg_taps:
            taps[name] = nc.dram_tensor("tap_" + name, list(shape), dtype,
                                        kind="ExternalOutput").ap()

    tapd("lnx", (128, 1024)); tapd("lnxT", (128, 8 * TT), BF16)
    tapd("mixed", (128, 24 * 515), BF16); tapd("g", (16, TT))
    tapd("spnb", (16, TT)); tapd("A", (16, TT)); tapd("K2", (16, TT))
    tapd("Q2", (16, TT)); tapd("Bt", (16, TT)); tapd("At", (16, TT))
    tapd("Dsb", (128, 2048), BF16); tapd("Psb", (128, 2048), BF16)
    tapd("vtok", (128, 1024), BF16); tapd("ktil", (128, 1024), BF16)
    tapd("qtil", (128, 1024), BF16)
    tapd("och", (128, 1024)); tapd("gated", (128, 1024), BF16)
    tapd("Sall", (128, 512)); tapd("x2", (128, 1024))
    tapd("wbig", (128, 8 * 1024), BF16); tapd("ztap", (128, 1024))
    tapd("eaq", (128, 1024), BF16)

    sb = lambda name, shape, dt=BF16: es.enter_context(nc.sbuf_tensor(name, shape, dt))
    wqkvT = sb("s_wqkvT", [128, 8 * CDIM])
    wzT = sb("s_wzT", [128, 8 * D])
    woutT = sb("s_woutT", [128, 8 * D])
    wabT = sb("s_wabT", [128, 8 * 48])
    wbig = sb("s_wbig", [128, 8 * D])
    dwin = sb("s_dwin", [128, 2 * 512])
    i128b = sb("s_i128b", [128, 128])
    itile4 = sb("s_itile4", [128, 512])
    masknegT = sb("s_mnegT", [128, 128])
    i16f = sb("s_i16f", [16, 16], F32)
    bonesb = sb("s_bones", [128, 8 * 16])
    biasb = sb("s_biasb", [33, 512])
    onesrow = sb("s_ones", [33, 128])
    dt_sb = sb("s_dt", [16, 1], F32)
    negea = sb("s_negea", [16, 1], F32)
    alog_sb = sb("s_alog", [16, 1], F32)
    ones16f = sb("s_ones16", [16, 128], F32)
    xa = [sb("s_xa0", [128, D], F32), sb("s_xa1", [128, D], F32)]
    xr = [sb("s_xr0", [128, D], F32), sb("s_xr1", [128, D], F32)]
    lnx = [sb("s_lnx0", [128, D]), sb("s_lnx1", [128, D])]
    lnxT = sb("s_lnxT", [128, 8 * TT])
    mixed = sb("s_mixed", [128, 24 * 515])
    halo = sb("s_halo", [128, 24 * 3])
    z_chunk = sb("s_z", [128, 1024], F32)
    vtok = sb("s_vtok", [128, 1024])
    ktil = sb("s_ktil", [128, 1024])
    qtil = sb("s_qtil", [128, 1024])
    ektok = sb("s_ektok", [128, 16], F32)
    eaq = sb("s_eaq", [128, 1024])
    dsb_t = sb("s_dsb", [128, 2048])
    o2 = dsb_t[:, 0:1024]
    gatedc = sb("s_gatedc", [128, 1024])
    gatedT = sb("s_gatedT", [128, 1024])
    ln2c = gatedc
    ln2T = gatedT
    x2tc = [sb("s_x2a", [128, 1024], F32), sb("s_x2b", [128, 1024], F32)]
    k2s = dsb_t[:, 0:512]
    K2cm = sb("s_K2cm", [16, TT], F32)
    Q2cm = sb("s_Q2cm", [16, TT], F32)
    g_cm = sb("s_gcm", [16, TT], F32)
    A_cm = sb("s_Acm", [16, TT], F32)
    Bt_cm = sb("s_Btcm", [16, TT], F32)
    ek_cm = sb("s_ekcm", [16, 128], F32)
    gam = sb("s_gam", [16, NCH], F32)
    gamb = sb("s_gamb", [128, 8 * NCH], F32)
    r1s = sb("s_r1s", [16, TT], F32)
    r2s = sb("s_r2s", [16, TT], F32)
    spnb = r2s
    At_cm = K2cm
    ea_cm = Q2cm
    b3 = sb("s_b3", [51, TT])
    a3 = sb("s_a3", [48, TT])
    ind51 = [sb("s_i51A", [51, 2048])]
    S_all = sb("s_Sall", [128, 512], F32)
    S_bf = sb("s_Sbf", [128, 512])
    rsq = sb("s_rsq", [128, 16], F32)
    rsqb = sb("s_rsqb", [128, 16], F32)
    stat_m = sb("s_statm", [128, 1], F32)
    stat_v = sb("s_statv", [128, 1], F32)
    stat_t = sb("s_statt", [128, 1], F32)
    bnbuf = sb("s_bn", [128, 14], F32)
    cstp = sb("s_cst", [128, 4], F32)
    markb = sb("s_mark", [1, 16])

    ps1 = es.enter_context(nc.psum_tensor("ps1", [128, 1024], F32))
    ps2 = es.enter_context(nc.psum_tensor("ps2", [128, 1024], F32))
    psS = es.enter_context(nc.psum_tensor("psS", [128, 1024], F32))
    psT = es.enter_context(nc.psum_tensor("psT", [128, 1024], BF16))

    sL = Sem(nc, es, "sL")
    sG = Sem(nc, es, "sG")
    sT = Sem(nc, es, "sT")
    sV = Sem(nc, es, "sV")
    sS = Sem(nc, es, "sS")
    sO = [Sem(nc, es, "sO0"), Sem(nc, es, "sO1")]
    s_xa = [Sem(nc, es, "sxa0"), Sem(nc, es, "sxa1")]
    s_xr = [Sem(nc, es, "sxr0"), Sem(nc, es, "sxr1")]
    s_dw = [Sem(nc, es, "sdw0"), Sem(nc, es, "sdw1")]
    s_i51 = Sem(nc, es, "si51")
    s_eaq = Sem(nc, es, "seaq")
    s_bnc = Sem(nc, es, "sbnc")
    s_spl = Sem(nc, es, "sspl")
    s_dg = Sem(nc, es, "sdg")
    s_dgm = Sem(nc, es, "sdgm")
    s_fb = [Sem(nc, es, "sfb0"), Sem(nc, es, "sfb1")]
    s_wz = Sem(nc, es, "swz")
    s_tap = Sem(nc, es, "stap")

    blk = es.enter_context(nc.Block())

    _DRAIN_OPS = {"activation", "copy", "add", "mul", "square", "sqrt",
                  "tensor_copy", "tensor_tensor", "tensor_scalar",
                  "scalar_tensor_tensor", "tensor_reduce", "bn_stats",
                  "bn_aggr", "memset", "tensor_tensor_scan", "reciprocal",
                  "select", "transpose", "sign"}
    _TRACK_OPS = _DRAIN_OPS | {"matmul", "ldweights"}

    class EngProxy:
        def __init__(self, eng, drain):
            self._e = eng
            self._drain = drain
            self.last = None
            self.dirty = False

        def __getattr__(self, n):
            a = getattr(self._e, n)
            if n in _TRACK_OPS:
                def wrapped(*args, **kw):
                    if self._drain:
                        self._e.drain()
                    r = a(*args, **kw)
                    self.last = r
                    self.dirty = True
                    return r
                return wrapped
            return a

    G_, SY = nc.gpsimd, nc.sync
    T = EngProxy(nc.tensor, drain=False)
    V_ = EngProxy(nc.vector, drain=drains)
    S_ = EngProxy(nc.scalar, drain=drains)
    sT.owner = T
    sV.owner = V_
    sS.owner = S_

    def tap(name, src_ap, eng=None):
        if name not in taps:
            return
        e = eng or G_
        sV.wait(e); sS.wait(e); sT.wait(e)
        s_tap.inc(e.dma_start(out=taps[name], in_=src_ap), 16)
        s_tap.wait(e)
        s_tap.wait(V_); s_tap.wait(S_); s_tap.wait(T)

    sG.inc(G_.dma_start(out=wqkvT[:].rearrange("p (dc c) -> p dc c", dc=8),
                        in_=d_wqkvT.rearrange("(dc p) c -> p dc c", p=128)), 16)
    sG.inc(G_.dma_start(out=wabT[:].rearrange("p (dc c) -> p dc c", dc=8),
                        in_=d_wabT.rearrange("(dc p) c -> p dc c", p=128)), 16)
    sG.inc(G_.dma_start(out=i128b[:], in_=d_i128), 16)
    sG.inc(G_.dma_start(out=itile4[:], in_=d_itile4), 16)
    sG.inc(G_.dma_start(out=masknegT[:], in_=d_mneg), 16)
    sG.inc(G_.dma_start(out=bonesb[:].rearrange("p (cc h) -> p cc h", cc=8),
                        in_=d_bones.rearrange("cc p h -> p cc h")), 16)
    sG.inc(G_.dma_start(out=biasb[0:33:32, :], in_=d_bias.rearrange("o (two c) -> (o two) c", two=2)), 16)
    sG.inc(G_.dma_start(out=ind51[0][0:48, :], in_=d_ind48), 16)
    sG.inc(G_.dma_start(out=b3[48:51, :], in_=d_ones3[:, 0:TT]), 16)
    sL.inc(SY.dma_start(out=i16f[:], in_=d_i16), 16)
    sL.inc(SY.dma_start(out=dt_sb[:], in_=d_dtb.rearrange("(h o) -> h o", o=1)), 16)
    sL.inc(SY.dma_start(out=alog_sb[:], in_=d_alog.rearrange("(h o) -> h o", o=1)), 16)

    sL.wait(S_)
    sS.inc(S_.activation(out=negea[:], in_=alog_sb[:], func=AF.Exp))
    sS.wait(V_)
    sG.wait(V_)
    sV.inc(V_.tensor_scalar(out=negea[:], in0=negea[:], scalar1=-1.0,
                            scalar2=None, op0=ALU.mult))
    for ci, cv in enumerate([1e-5, 1.0, 1e-6, -LN8]):
        V_.memset(cstp[:, ci:ci + 1], cv)
        nc.const_aps.register(F32, cv, cstp[:, ci:ci + 1])
    V_.memset(S_all[:], 0.0)
    V_.memset(S_bf[:], 0.0)
    V_.memset(onesrow[:], 1.0)
    V_.memset(halo[:], 0.0)
    V_.memset(mixed[:], 0.0)
    sV.inc(V_.memset(ones16f[:], 1.0))

    diag_flat = d_diag.rearrange("cc tap p c -> (cc tap) p c")
    diagbf_flat = d_diagbf.rearrange("cc tap p c -> (cc tap) p c")
    for grp in range(6):
        p0 = grp * 16
        if grp > 0:
            s_dgm.wait(G_)
        s_dg.inc(G_.dma_start(
            out=dsb_t[:].rearrange("p (f c) -> p f c", f=16),
            in_=diag_flat[p0:p0 + 16].rearrange("f p c -> p f c")), 16)
        s_dg.wait(SY)
        s_dgm.inc(SY.dma_start(
            out=diagbf_flat[p0:p0 + 16].rearrange("f p c -> p f c"),
            in_=dsb_t[:].rearrange("p (f c) -> p f c", f=16)), 16)

    ffnbuf = [(z_chunk[:, 0:512], z_chunk[:, 512:1024]),
              (x2tc[1][:, 0:512], x2tc[1][:, 512:1024])]
    ffnACC = x2tc[0][:, 0:512]
    ffnTMP = x2tc[0][:, 512:1024]
    steps = [(dp1, dp2, kind, rc, cidx)
             for (dp1, dp2, kind) in ((d_p1r, d_p2r, "r"), (d_p1i, d_p2i, "i"))
             for rc in range(4) for cidx in range(4)]
    g_done = [0] * len(steps)
    v_done = [0] * len(steps)

    def ffn_issue(s):
        dp1, dp2, kind, rc, cidx = steps[s]
        b = s % 2
        if s >= 2:
            sV.wait(G_, v_done[s - 2])
        s_fb[b].inc(G_.dma_start(out=ffnbuf[b][0],
                            in_=dp1[cidx, rc * 128:(rc + 1) * 128, :]), 16)
        s_fb[b].inc(G_.dma_start(out=ffnbuf[b][1],
                            in_=dp2[cidx, rc * 128:(rc + 1) * 128, :]), 16)
        g_done[s] = s_fb[b].v

    ffn_issue(0)
    ffn_issue(1)
    for s, (dp1, dp2, kind, rc, cidx) in enumerate(steps):
        b = s % 2
        s_fb[b].wait(V_, g_done[s])
        if cidx == 0:
            sV.inc(V_.tensor_tensor(out=ffnACC, in0=ffnbuf[b][0],
                                    in1=ffnbuf[b][1], op=ALU.mult))
        else:
            sV.inc(V_.tensor_tensor(out=ffnTMP, in0=ffnbuf[b][0],
                                    in1=ffnbuf[b][1], op=ALU.mult))
            sV.inc(V_.tensor_tensor(out=ffnACC, in0=ffnACC,
                                    in1=ffnTMP, op=ALU.add))
        v_done[s] = sV.fence()
        if s + 2 < len(steps):
            ffn_issue(s + 2)
        if cidx == 3:
            if kind == "r":
                sV.inc(V_.tensor_copy(out=wbig[:, rc * 1024:rc * 1024 + 512],
                                      in_=ffnACC))
                sV.inc(V_.tensor_copy(out=wbig[:, (4 + rc) * 1024 + 512:(4 + rc) * 1024 + 1024],
                                      in_=ffnACC))
            else:
                sV.inc(V_.tensor_copy(out=wbig[:, rc * 1024 + 512:rc * 1024 + 1024],
                                      in_=ffnACC))
                sV.inc(V_.tensor_scalar(out=wbig[:, (4 + rc) * 1024:(4 + rc) * 1024 + 512],
                                        in0=ffnACC, scalar1=-1.0, scalar2=None,
                                        op0=ALU.mult))
    tap("wbig", wbig[:])

    s_wz.inc(G_.dma_start(out=wzT[:].rearrange("p (dc c) -> p dc c", dc=8),
                        in_=d_wzT.rearrange("(dc p) c -> p dc c", p=128)), 16)
    s_wz.inc(G_.dma_start(out=woutT[:].rearrange("p (dc c) -> p dc c", dc=8),
                        in_=d_woutT.rearrange("(dc p) c -> p dc c", p=128)), 16)
    wz_ready = s_wz.v
    s_dgm.wait(V_)

    total_tiles = (n_pre + n_own) if stage >= 2 else 0
    xa_free = [0, 0]
    xr_free = [0, 0]
    diag_loads = [0, 0]

    for tile in range(total_tiles):
        pre = tile < n_pre
        last_pre = pre and (tile == n_pre - 1)
        if tile == n_pre:
            s_wz.wait(T, wz_ready)
        xsrc = xpre if pre else xown
        row0 = (tile if pre else tile - n_pre) * TT
        cc_list = list(range(8, 24)) if pre else list(range(24))

        for tc in range(4):
            SY.wait_ge(sV.h, xa_free[tc % 2])
            s_xa[tc % 2].inc(SY.dma_start(out=xa[tc % 2][:],
                                in_=xsrc[row0 + tc * 128: row0 + (tc + 1) * 128]), 16)
            xs = xa[tc % 2][:]
            s_xa[tc % 2].wait(V_)
            sT.wait(V_)
            V_.bn_stats(out=bnbuf[:, 0:6], in_=xs[:, 0:512])
            V_.bn_stats(out=bnbuf[:, 6:12], in_=xs[:, 512:1024])
            sV.inc(V_.bn_aggr(out=bnbuf[:, 12:14], in_=bnbuf[:, 0:12]))
            sV.wait(S_)
            S_.activation(out=stat_t[:], in_=bnbuf[:, 13:14], func=AF.Ln, bias=1e-5)
            sS.inc(S_.activation(out=stat_v[:], in_=stat_t[:], func=AF.Exp, scale=-0.5))
            sS.wait(V_)
            lx = lnx[tc % 2]
            sV.inc(V_.scalar_tensor_tensor(
                out=lx[:], in0=xs, scalar=bnbuf[:, 12:13], op0=ALU.subtract,
                in1=stat_v[:].broadcast_to([128, D]), op1=ALU.mult))
            xa_free[tc % 2] = sV.fence()
            if tc == 0:
                tap("lnx", lx[:])
            sV.wait(T)
            for dc in range(8):
                sT.inc(T.transpose(out=psT[:, dc * 128:(dc + 1) * 128],
                                   in_=lx[:, dc * 128:(dc + 1) * 128],
                                   identity=i128b[:]))
            sT.wait(V_)
            sV.inc(V_.tensor_copy(
                out=lnxT[:].rearrange("p (dc t) -> p dc t", dc=8)[:, :, tc * 128:(tc + 1) * 128],
                in_=psT[:].rearrange("p (dc t) -> p dc t", dc=8)))
            sV.wait(T)
        tap("lnxT", lnxT[:])

        evac_cnt = {}
        for i_cc, cc in enumerate(cc_list):
            q = i_cc % 2
            pq = ps1[:, q * 512:(q + 1) * 512]
            if i_cc >= 2:
                w = evac_cnt[i_cc - 2]
                T.wait_ge(w[0].h, w[1])
            for dc in range(8):
                sT.inc(T.matmul(pq, wqkvT[:, dc * CDIM + cc * 128: dc * CDIM + (cc + 1) * 128],
                                lnxT[:, dc * TT:(dc + 1) * TT],
                                start=(dc == 0), stop=(dc == 7)))
            eng, sem = (V_, sV) if (i_cc % 2 == 0) else (S_, sS)
            sT.wait(eng)
            if eng is V_:
                sem.inc(eng.tensor_copy(out=mixed[:, cc * 515 + 3: cc * 515 + 515], in_=pq))
            else:
                sem.inc(eng.copy(out=mixed[:, cc * 515 + 3: cc * 515 + 515], in_=pq))
            evac_cnt[i_cc] = (sem, sem.fence())
        if last_pre:
            sV.wait(T)
            sS.wait(T)
            for cc in range(8):
                q2h = ps1[:, 0:128] if cc % 2 == 0 else ps1[:, 512:640]
                if cc >= 2:
                    w = evac_cnt[("qh", cc - 2)]
                    T.wait_ge(w[0].h, w[1])
                for dc in range(8):
                    sT.inc(T.matmul(q2h, wqkvT[:, dc * CDIM + cc * 128: dc * CDIM + (cc + 1) * 128],
                                    lnxT[:, dc * TT + 384: dc * TT + 512],
                                    start=(dc == 0), stop=(dc == 7)))
                sT.wait(V_)
                sV.inc(V_.tensor_copy(out=mixed[:, cc * 515 + 3 + 384: cc * 515 + 515], in_=q2h))
                evac_cnt[("qh", cc)] = (sV, sV.fence())
        sS.wait(V_)
        sV.inc(V_.tensor_copy(
            out=halo[:].rearrange("p (cc c) -> p cc c", cc=24),
            in_=mixed[:].rearrange("p (cc c) -> p cc c", cc=24)[:, :, 512:515]))

        sV.wait(T)
        sS.wait(T)
        for dc in range(8):
            sT.inc(T.matmul(psS[0:48, 0:512], wabT[:, dc * 48:(dc + 1) * 48],
                            lnxT[:, dc * TT:(dc + 1) * TT],
                            start=(dc == 0), stop=(dc == 7)))
        sT.wait(S_)
        S_.activation(out=r2s[:], in_=psS[0:16, 0:512], func=AF.Exp, bias=dt_sb[:])
        sS.inc(S_.activation(out=r1s[:], in_=r2s[:], func=AF.Ln, bias=1.0))
        sS.wait(V_)
        sV.inc(V_.tensor_scalar(out=g_cm[:], in0=r1s[:], scalar1=negea[:],
                                scalar2=None, op0=ALU.mult))
        tap("g", g_cm[:])
        sV.wait(S_)
        S_.activation(out=r1s[:], in_=psS[32:48, 0:512], func=AF.Exp, scale=-1.0)
        sS.inc(S_.activation(out=spnb[:], in_=r1s[:], func=AF.Ln, bias=1.0))
        sS.wait(T)
        tap("spnb", spnb[:])

        for i_cc, cc in enumerate(cc_list):
            dhalf = i_cc % 2
            if i_cc >= 2:
                w = evac_cnt.get(("conv", i_cc - 2))
                SY.wait_ge(w[0].h, w[1])
            s_dw[dhalf].inc(SY.dma_start(
                out=dwin[:, dhalf * 512:(dhalf + 1) * 512].rearrange("p (f c) -> p f c", f=4),
                in_=diagbf_flat[cc * 4:(cc + 1) * 4].rearrange("f p c -> p f c")), 16)
            s_dw[dhalf].wait(T)
            sV.wait(T)
            sS.wait(T)
            ph = ps2[:, dhalf * 512:(dhalf + 1) * 512]
            for tapn in range(4):
                sT.inc(T.matmul(ph, dwin[:, dhalf * 512 + tapn * 128: dhalf * 512 + (tapn + 1) * 128],
                                mixed[:, cc * 515 + tapn: cc * 515 + tapn + 512],
                                start=(tapn == 0), stop=(tapn == 3)))
            evac_cnt[("conv", i_cc)] = (sT, sT.fence())
            sT.wait(S_)
            sS.inc(S_.activation(out=mixed[:, cc * 515 + 3: cc * 515 + 515], in_=ph,
                                 func=AF.Silu))
        sS.wait(V_)
        sT.wait(V_)
        sV.inc(V_.tensor_copy(
            out=mixed[:].rearrange("p (cc c) -> p cc c", cc=24)[:, :, 0:3],
            in_=halo[:].rearrange("p (cc c) -> p cc c", cc=24)))
        tap("mixed", mixed[:])

        if stage < 3:
            continue
        def sumsq(cc0, dest_cm, tapname):
            sS.wait(T)
            sV.wait(T)
            for i in range(8):
                cc = cc0 + i
                sS.wait(V_)
                if i > 0:
                    sT.wait(V_)
                sV.inc(V_.tensor_tensor(out=k2s[:], in0=mixed[:, cc * 515 + 3: cc * 515 + 515],
                                        in1=mixed[:, cc * 515 + 3: cc * 515 + 515], op=ALU.mult))
                sV.wait(T)
                sT.inc(T.matmul(psS[0:16, 0:512], bonesb[:, i * 16:(i + 1) * 16], k2s[:],
                                start=(i == 0), stop=(i == 7)))
            sT.wait(V_)
            sV.inc(V_.tensor_copy(out=dest_cm[:], in_=psS[0:16, 0:512]))
            tap(tapname, dest_cm[:])
        sumsq(8, K2cm, "K2")
        if not pre:
            sumsq(0, Q2cm, "Q2")

        sS.wait(V_); sG.wait(V_)
        for ch in range(NCH):
            sV.inc(V_.tensor_tensor_scan(out=A_cm[:, ch * 128:(ch + 1) * 128],
                                         data0=ones16f[:], data1=g_cm[:, ch * 128:(ch + 1) * 128],
                                         initial=0.0, op0=ALU.mult, op1=ALU.add))
        tap("A", A_cm[:])
        sV.wait(S_)
        sS.inc(S_.activation(out=r1s[:], in_=K2cm[:], func=AF.Ln, bias=1e-6))
        sS.wait(V_)
        V_.scalar_tensor_tensor(out=Bt_cm[:], in0=r1s[:], scalar=0.5, op0=ALU.mult,
                                in1=A_cm[:], op1=ALU.add)
        sV.inc(V_.tensor_tensor(out=Bt_cm[:], in0=Bt_cm[:], in1=spnb[:], op=ALU.add))
        tap("Bt", Bt_cm[:])
        if not pre:
            sV.wait(S_)
            sS.inc(S_.activation(out=r2s[:], in_=Q2cm[:], func=AF.Ln, bias=1e-6))
            sS.wait(V_)
            sV.inc(V_.scalar_tensor_tensor(out=At_cm[:], in0=r2s[:], scalar=-0.5,
                                           op0=ALU.mult, in1=A_cm[:], op1=ALU.add))
            tap("At", At_cm[:])
        gcol_slice = A_cm[:].rearrange("h (ch t) -> h ch t", ch=NCH)[:, :, 127:128]
        sV.wait(S_)
        sS.inc(S_.activation(out=gam[:], in_=gcol_slice.rearrange("h ch o -> h (ch o)"),
                             func=AF.Exp))
        tmpA = dsb_t[0:16, 0:512]
        tmpB = dsb_t[0:16, 512:1024]
        def split48(srcx, dst, rscr):
            sS.wait(V_)
            sT.wait(V_)
            sV.inc(V_.tensor_copy(out=dst[0:16, :], in_=srcx[:]))
            sV.inc(V_.scalar_tensor_tensor(out=rscr[:], in0=srcx[:], scalar=1.0,
                                           op0=ALU.mult, in1=dst[0:16, :], op1=ALU.subtract))
            sV.inc(V_.tensor_copy(out=tmpA, in_=rscr[:]))
            sV.inc(V_.scalar_tensor_tensor(out=rscr[:], in0=rscr[:], scalar=1.0,
                                           op0=ALU.mult, in1=tmpA, op1=ALU.subtract))
            sV.inc(V_.tensor_copy(out=tmpB, in_=rscr[:]))
            sV.wait(G_)
            s_spl.inc(G_.dma_start(out=dst[16:32, :], in_=tmpA), 16)
            s_spl.inc(G_.dma_start(out=dst[32:48, :], in_=tmpB), 16)
            s_spl.wait(V_)
            s_spl.wait(T)
        split48(Bt_cm, b3, r1s)
        if not pre:
            split48(At_cm, a3, r1s)
            sV.wait(S_)
            sS.inc(S_.activation(out=ea_cm[:], in_=At_cm[:], func=AF.Exp, bias=-LN8))
            sS.wait(G_)
            s_bnc.inc(G_.dma_start(out=d_scr[0:8192].rearrange("(h t) -> h t", h=16),
                                   in_=ea_cm[:]), 16)
        sS.wait(G_)
        sV.wait(G_)
        s_bnc.inc(G_.dma_start(out=d_scr[8192:8192 + 16 * NCH].rearrange("(h c) -> h c", h=16),
                               in_=gam[:]), 16)
        s_bnc.wait(G_)
        gfull = d_scr[8192:8192 + 16 * NCH].rearrange("(h ch) -> h ch", h=16)
        for ph in range(2):
            gsrc = gfull[ph:16:2][None, :, :].broadcast_to([64, 8, NCH])
            s_bnc.inc(G_.dma_start(
                out=gamb[ph * 64:(ph + 1) * 64, :].rearrange("p (pair ch) -> p pair ch", pair=8),
                in_=gsrc), 16)
        s_bnc.wait(V_)

        if stage < 4:
            continue
        for ch in range(NCH):
            tc = ch
            col = ch * 128
            if not pre:
                SY.wait_ge(sV.h, xr_free[ch % 2])
                s_xr[ch % 2].inc(SY.dma_start(out=xr[ch % 2][:],
                                    in_=xsrc[row0 + tc * 128: row0 + (tc + 1) * 128]), 16)
            if not pre:
                i51 = ind51[0]
                sV.wait(G_)
                sT.wait(G_)
                G_.wait_ge(s_spl.h, s_spl.v)
                for s3 in range(3):
                    s_i51.inc(G_.dma_start(
                        out=i51[48 + s3:49 + s3, :].rearrange("o (h c) -> o h c", h=16),
                        in_=a3[16 * s3:16 * (s3 + 1), col:col + 128]), 16)
                efull = d_scr[0:8192].rearrange("(h t) -> h t", h=16)
                G_.wait_ge(s_bnc.h, s_bnc.v)
                for ph in range(2):
                    esrc = efull[ph:16:2, col:col + 128]
                    esrc = esrc[None, :, :].broadcast_to([64, 8, 128])
                    s_eaq.inc(G_.dma_start(
                        out=eaq[ph * 64:(ph + 1) * 64, :].rearrange("p (cc i) -> p cc i", cc=8),
                        in_=esrc), 16)

            sV.wait(T)
            for vcc in range(8):
                cc = 16 + vcc
                sT.inc(T.transpose(out=psT[:, vcc * 128:(vcc + 1) * 128],
                                   in_=mixed[:, cc * 515 + 3 + col: cc * 515 + 3 + col + 128],
                                   identity=i128b[:]))
            sT.wait(V_)
            sV.inc(V_.tensor_copy(out=vtok[:], in_=psT[:]))
            if ch == 0:
                tap("vtok", vtok[:])
            sV.wait(S_)
            sS.inc(S_.activation(out=ek_cm[:], in_=Bt_cm[:, col:col + 128], func=AF.Exp,
                                 scale=-1.0,
                                 bias=A_cm[:].rearrange("h (c t) -> h c t", c=NCH)[:, ch, 127:128]))
            sS.wait(T)
            sV.wait(T)
            sT.inc(T.matmul(psS[:, 0:16], ek_cm[:], i16f[:], start=True, stop=True))
            sT.wait(V_)
            sV.inc(V_.tensor_copy(out=ektok[:], in_=psS[:, 0:16]))
            sV.wait(T)
            for kcc in range(8):
                cc = 8 + kcc
                sT.inc(T.transpose(out=psT[:, kcc * 128:(kcc + 1) * 128],
                                   in_=mixed[:, cc * 515 + 3 + col: cc * 515 + 3 + col + 128],
                                   identity=i128b[:]))
            sT.wait(V_)
            sV.inc(V_.tensor_tensor(
                out=ktil[:].rearrange("p (h w) -> p h w", w=64),
                in0=psT[:].rearrange("p (h w) -> p h w", w=64),
                in1=ektok[:, :, None].broadcast_to([128, 16, 64]),
                op=ALU.mult))
            if ch == 0:
                tap("ktil", ktil[:])

            if stage < 41:
                continue
            if not pre:
                s_eaq.wait(V_)
                qsl = mixed[:].rearrange("p (cc c) -> p cc c", cc=24)[:, 0:8, 3 + col: 3 + col + 128]
                sV.inc(V_.tensor_tensor(out=qtil[:].rearrange("p (cc i) -> p cc i", cc=8),
                                        in0=qsl, in1=eaq[:].rearrange("p (cc i) -> p cc i", cc=8),
                                        op=ALU.mult))
                if ch == 0:
                    tap("qtil", qtil[:])
                if stage < 42:
                    continue
                s_i51.wait(T)
                for h2 in range(2):
                    sS.wait(T)
                    sV.wait(T)
                    ihalf = i51[:, h2 * 1024:(h2 + 1) * 1024].rearrange(
                        "k (ql two c) -> k two ql c", ql=4, two=2, c=128)
                    for par in range(2):
                        pq = ps1[:, par * 512:(par + 1) * 512]
                        sT.inc(T.matmul(pq, b3[:, col:col + 128],
                                        ihalf[:, par, :, :],
                                        start=True, stop=False))
                        sT.inc(T.matmul(pq, masknegT[:], itile4[:],
                                        start=False, stop=True))
                    if stage >= 43:
                        sT.wait(S_)
                        sS.inc(S_.activation(out=dsb_t[:, h2 * 1024:(h2 + 1) * 1024],
                                             in_=ps1[:], func=AF.Exp, bias=-LN8))
                    if stage >= 44:
                        sS.wait(T)
                        for ql in range(4):
                            qcc = h2 * 4 + ql
                            kl = mixed[:, (8 + qcc) * 515 + 3 + col: (8 + qcc) * 515 + 3 + col + 128]
                            qll = mixed[:, qcc * 515 + 3 + col: qcc * 515 + 3 + col + 128]
                            sT.inc(T.matmul(ps1[:, ql * 128:(ql + 1) * 128],
                                            kl[0:64, :], qll[0:64, :],
                                            start=True, stop=True, tile_position=(0, 0)))
                            sT.inc(T.matmul(ps1[:, 512 + ql * 128:512 + (ql + 1) * 128],
                                            kl[64:128, :], qll[64:128, :],
                                            start=True, stop=True, tile_position=(64, 0)))
                    if stage >= 45:
                        sT.wait(V_)
                        sV.inc(V_.tensor_tensor(out=dsb_t[:, h2 * 1024:(h2 + 1) * 1024],
                                                in0=ps1[:],
                                                in1=dsb_t[:, h2 * 1024:(h2 + 1) * 1024],
                                                op=ALU.mult))
                if ch == 0:
                    tap("Dsb", dsb_t[:], eng=G_)
                    tap("Psb", dsb_t[:], eng=G_)
                if stage < 46:
                    continue
                sV.wait(T)
                for dc in range(8):
                    lslice = lnxT[:, dc * TT + col: dc * TT + col + 128]
                    sT.inc(T.matmul(ps2[:, 0:512], lslice, wzT[:, dc * D: dc * D + 512],
                                    start=(dc == 0), stop=(dc == 7)))
                    sT.inc(T.matmul(ps2[:, 512:1024], lslice, wzT[:, dc * D + 512: (dc + 1) * D],
                                    start=(dc == 0), stop=(dc == 7)))
                sT.wait(S_)
                sS.inc(S_.activation(out=z_chunk[:], in_=ps2[:], func=AF.Silu))
                if ch == 0:
                    tap("ztap", z_chunk[:])
                if stage < 47:
                    continue
                sS.wait(T)
                for hh in range(16):
                    par, pair = hh % 2, hh // 2
                    dcol = (hh // 8) * 1024 + par * 512 + ((hh % 8) // 2) * 128
                    sT.inc(T.matmul(ps2[:, par * 512 + pair * 64: par * 512 + (pair + 1) * 64],
                                    dsb_t[:, dcol:dcol + 128],
                                    vtok[:, hh * 64:(hh + 1) * 64],
                                    start=(hh < 2), stop=False))
                for qcc in range(8):
                    sT.inc(T.matmul(ps2[:, qcc * 64:(qcc + 1) * 64],
                                    qtil[0:64, qcc * 128:(qcc + 1) * 128],
                                    S_bf[0:64, qcc * 64:(qcc + 1) * 64],
                                    start=False, stop=(qcc == 7), tile_position=(0, 0)))
                    sT.inc(T.matmul(ps2[:, 512 + qcc * 64:512 + (qcc + 1) * 64],
                                    qtil[64:128, qcc * 128:(qcc + 1) * 128],
                                    S_bf[64:128, qcc * 64:(qcc + 1) * 64],
                                    start=False, stop=(qcc == 7),
                                    tile_position=(64, 0)))
                o_done = sT.v

            if stage < 48:
                continue
            sV.wait(T)
            for hh in range(16):
                tp = (0, 0) if hh % 2 == 0 else (0, 64)
                rb = 0 if hh % 2 == 0 else 64
                cb = 0 if hh % 2 == 0 else 512
                pair = hh // 2
                sT.inc(T.matmul(psS[rb:rb + 64, cb + pair * 64: cb + (pair + 1) * 64],
                                ktil[:, hh * 64:(hh + 1) * 64],
                                vtok[:, hh * 64:(hh + 1) * 64],
                                start=(hh < 2), stop=(hh >= 14), tile_position=tp))
            state_done = sT.v

            if stage < 49:
                continue
            if not pre:
                sT.wait(S_)
                sS.inc(S_.activation(out=o2, in_=ps2[:], func=AF.Square))
                sS.wait(V_)
                sV.inc(V_.tensor_reduce(out=rsq[:], in_=o2.rearrange("p (h w) -> p h w", w=64),
                                        axis=AX.X, op=ALU.add))
                sV.wait(S_)
                S_.activation(out=rsqb[:], in_=rsq[:], func=AF.Ln, scale=1.0 / 64, bias=1e-6)
                sS.inc(S_.activation(out=rsqb[:], in_=rsqb[:], func=AF.Exp, scale=-0.5))
                sS.wait(V_)
                V_.scalar_tensor_tensor(
                    out=gatedc[:].rearrange("p (h w) -> p h w", w=64),
                    in0=ps2[:].rearrange("p (h w) -> p h w", w=64),
                    scalar=1.0, op0=ALU.mult,
                    in1=rsqb[:, :, None].broadcast_to([128, 16, 64]),
                    op1=ALU.mult)
                sV.inc(V_.tensor_tensor(out=gatedc[:], in0=gatedc[:], in1=z_chunk[:],
                                        op=ALU.mult))
                if ch == 0:
                    tap("gated", gatedc[:])

            if stage < 50:
                continue
            sT.wait(V_)
            gslice = gamb[:, ch: 8 * NCH: NCH]
            V_.tensor_tensor(out=S_all[:].rearrange("p (pair w) -> p pair w", pair=8),
                             in0=S_all[:].rearrange("p (pair w) -> p pair w", pair=8),
                             in1=gslice[:, :, None].broadcast_to([128, 8, 64]),
                             op=ALU.mult)
            V_.tensor_tensor(out=S_all[0:64, :], in0=S_all[0:64, :],
                             in1=psS[0:64, 0:512], op=ALU.add)
            V_.tensor_tensor(out=S_all[64:128, :], in0=S_all[64:128, :],
                             in1=psS[64:128, 512:1024], op=ALU.add)
            sV.inc(V_.tensor_copy(out=S_bf[:], in_=S_all[:]))
            if pre and ch == NCH - 1 and tile == n_pre - 1:
                tap("Sall", S_all[:])

            if stage < 51:
                continue
            if not pre:
                sV.wait(T)
                for dc in range(8):
                    sT.inc(T.transpose(out=psT[:, dc * 128:(dc + 1) * 128],
                                       in_=gatedc[:, dc * 128:(dc + 1) * 128],
                                       identity=i128b[:]))
                sT.wait(V_)
                sV.inc(V_.tensor_copy(out=gatedT[:], in_=psT[:]))
                sV.wait(T)
                sS.wait(T)
                for dc in range(8):
                    gsl = gatedT[:, dc * 128:(dc + 1) * 128]
                    sT.inc(T.matmul(ps2[:, 0:512], gsl, woutT[:, dc * D: dc * D + 512],
                                    start=(dc == 0), stop=(dc == 7)))
                    sT.inc(T.matmul(ps2[:, 512:1024], gsl, woutT[:, dc * D + 512:(dc + 1) * D],
                                    start=(dc == 0), stop=(dc == 7)))
                x2 = x2tc[ch % 2]
                sO[ch % 2].wait(V_)
                s_xr[ch % 2].wait(V_)
                sT.wait(V_)
                sV.inc(V_.tensor_tensor(out=x2[:], in0=ps2[:], in1=xr[ch % 2][:],
                                        op=ALU.add))
                xr_free[ch % 2] = sV.fence()
                if ch == 0:
                    tap("x2", x2[:])
                if stage < 52:
                    continue
                sV.wait(V_)
                V_.bn_stats(out=bnbuf[:, 0:6], in_=x2[:, 0:512])
                V_.bn_stats(out=bnbuf[:, 6:12], in_=x2[:, 512:1024])
                sV.inc(V_.bn_aggr(out=bnbuf[:, 12:14], in_=bnbuf[:, 0:12]))
                sV.wait(S_)
                S_.activation(out=stat_t[:], in_=bnbuf[:, 13:14], func=AF.Ln, bias=1e-5)
                sS.inc(S_.activation(out=stat_v[:], in_=stat_t[:], func=AF.Exp, scale=-0.5))
                sS.wait(V_)
                sV.inc(V_.scalar_tensor_tensor(
                    out=ln2c[:], in0=x2[:], scalar=bnbuf[:, 12:13], op0=ALU.subtract,
                    in1=stat_v[:].broadcast_to([128, D]), op1=ALU.mult))
                sV.wait(T)
                for dc in range(8):
                    sT.inc(T.transpose(out=psT[:, dc * 128:(dc + 1) * 128],
                                       in_=ln2c[:, dc * 128:(dc + 1) * 128],
                                       identity=i128b[:]))
                sT.wait(V_)
                sV.inc(V_.tensor_copy(out=ln2T[:], in_=psT[:]))
                if stage < 53:
                    continue
                sV.wait(T)
                for dg in range(8):
                    lsl = ln2T[:, dg * 128:(dg + 1) * 128]
                    sT.inc(T.matmul(ps2[:, 0:512], lsl, wbig[:, dg * D: dg * D + 512],
                                    start=(dg == 0), stop=False))
                    sT.inc(T.matmul(ps2[:, 512:1024], lsl, wbig[:, dg * D + 512:(dg + 1) * D],
                                    start=(dg == 0), stop=False))
                sT.inc(T.matmul(ps2[:, 0:512], onesrow[0:1, :], biasb[0:1, :],
                                start=False, stop=True))
                sT.inc(T.matmul(ps2[:, 512:1024], onesrow[32:33, :], biasb[32:33, :],
                                start=False, stop=True))
                sT.wait(V_)
                sV.inc(V_.tensor_tensor(out=x2[:], in0=ps2[:], in1=x2[:], op=ALU.add))
                sV.wait(SY)
                sO[ch % 2].inc(SY.dma_start(
                    out=y[row0 + tc * 128: row0 + (tc + 1) * 128, :], in_=x2[:]), 16)

    sO[0].wait(SY)
    sO[1].wait(SY)
    G_.wait_ge(sG.h, sG.v)
    G_.wait_ge(s_tap.h, s_tap.v)
    G_.wait_ge(s_bnc.h, s_bnc.v)
    G_.wait_ge(s_spl.h, s_spl.v)
    G_.wait_ge(s_i51.h, s_i51.v)
    G_.wait_ge(s_eaq.h, s_eaq.v)
    SY.wait_ge(s_dg.h, s_dg.v)
    SY.wait_ge(s_dgm.h, s_dgm.v)
    SY.wait_ge(s_fb[0].h, s_fb[0].v)
    SY.wait_ge(s_fb[1].h, s_fb[1].v)
    SY.wait_ge(s_wz.h, s_wz.v)
    SY.wait_ge(s_dw[0].h, s_dw[0].v)
    SY.wait_ge(s_dw[1].h, s_dw[1].v)
    SY.wait_ge(s_xa[0].h, s_xa[0].v)
    SY.wait_ge(s_xa[1].h, s_xa[1].v)
    SY.wait_ge(s_xr[0].h, s_xr[0].v)
    SY.wait_ge(s_xr[1].h, s_xr[1].v)
    V_._e.wait_ge(sL.h, sL.v)
    es.close()
    return nc, taps


from concourse.bass_utils import run_bass_kernel_spmd

N_PRE, N_OWN = 4, 4


def kernel(**inputs):
    in_maps = host_prep(inputs, n_pre=N_PRE, n_own=N_OWN)
    nc, _ = build_program(n_pre=N_PRE, n_own=N_OWN)
    res = run_bass_kernel_spmd(nc, in_maps, core_ids=list(range(8)))
    out = np.empty((4, 4096, 1024), np.float32)
    for core in range(8):
        b, half = core // 2, core % 2
        out[b, half * 2048:(half + 1) * 2048] = res.results[core]["y"]
    return out
